# version 20
# baseline (speedup 1.0000x reference)
import math
import contextlib
import numpy as np
import ml_dtypes
import concourse.bass as bass
import concourse.mybir as mybir
from concourse.bass_utils import run_bass_kernel_spmd

F32 = mybir.dt.float32
BF16 = mybir.dt.bfloat16
U8 = mybir.dt.uint8
AF = mybir.ActivationFunctionType
ALU = mybir.AluOpType
AX = mybir.AxisListType

D = 4096
S = 2048
NH = 32
HD = 128
QKVC = 4352
NE = 3277
HALF = 1024
NT = 8
NCH = 32
EPS = 1e-5
SCALE = float(np.float32(HD ** -0.5))
NEG = -1e30
EI = -4.0

_TAPS = ()
_PHASES = ("P0", "P1", "P2", "P3", "P35", "P4a", "P4aT", "P4b", "P4bT", "P4c", "P5")
_HALVES = 2
_PF = False


def _slopes():
    base = 2.0 ** (-2.0 ** (-(math.log2(32) - 3)))
    return [float(np.float32(base ** (h + 1))) for h in range(NH)]


SLOPES = _slopes()


def _tier(h):
    if h < 4:
        return 64
    if h < 8:
        return 128
    if h < 12:
        return 256
    return 512


ATHR = 1e18


def _ktmin(h, qq):
    lo = (qq * 512 - 127 - ATHR / SLOPES[h]) / 128.0
    return max(0, int(math.ceil(lo)))


def _subtiles(h, qq, kt):
    qs = max(0, kt * 128 - qq * 512)
    W = 512 - qs
    T = _tier(h)
    qg0 = qq * 512 + qs
    qg_end = (qq + 1) * 512
    subs = []
    q = qg0
    while q < qg_end:
        q_hi = min(qg_end, (q // T + 1) * T)
        subs.append((q - qg0, q_hi - q, kt * 128 - (q_hi - 1)))
        q = q_hi
    return qs, W, subs


def _plan_ebias():
    cols = {}
    for h in range(NH):
        for qq in range(4):
            for kt in range(_ktmin(h, qq), 4 * qq + 4):
                for _, _, delta in _subtiles(h, qq, kt)[2]:
                    cols.setdefault((h, delta), len(cols))
    tab = np.zeros((128, max(1, len(cols))), dtype=np.float32)
    p = np.arange(128, dtype=np.float64)
    for (h, delta), c in cols.items():
        tab[:, c] = (SLOPES[h] * (p + delta)).astype(np.float32)
    return cols, tab


EB_COLS, EB_TAB = _plan_ebias()


def build_program():
    nc = bass.Bass("TRN2", target_bir_lowering=False)

    x_h = nc.dram_tensor("x", [S, D], F32, kind="ExternalInput").ap()
    wqkv_h = nc.dram_tensor("wqkv", [D, QKVC], F32, kind="ExternalInput").ap()
    wo_h = nc.dram_tensor("wo", [D, D], F32, kind="ExternalInput").ap()
    k_h = [nc.dram_tensor(f"k{i}", [D, D], U8, kind="ExternalInput").ap()
           for i in range(3)]
    sv_h = nc.dram_tensor("svals", [1, 4], F32, kind="ExternalInput").ap()
    id_h = nc.dram_tensor("ident", [128, 128], BF16, kind="ExternalInput").ap()
    on_h = nc.dram_tensor("ones128", [128, 128], BF16, kind="ExternalInput").ap()
    onr_h = nc.dram_tensor("onesrow", [1, 128], F32, kind="ExternalInput").ap()
    tri_h = nc.dram_tensor("tri", [128, 128], BF16, kind="ExternalInput").ap()
    cs_h = nc.dram_tensor("cs", [128, NCH + 2], F32, kind="ExternalInput").ap()
    eb_h = nc.dram_tensor("ebias", [128, EB_TAB.shape[1]], F32,
                          kind="ExternalInput").ap()
    x1sp = nc.dram_tensor("x1sp", [S, D], BF16, kind="Internal").ap()
    out_h = nc.dram_tensor("out", [S, D], F32, kind="ExternalOutput").ap()

    taps = {}
    for t in _TAPS:
        shp = {"xnT": [128, NCH * HALF], "q": [128, NT * D], "kt": [128, S],
               "v": [128, 16 * 128], "attnT": [128, NCH * HALF],
               "y": [128, NT * D], "x1": [128, NT * D],
               "znT": [128, NCH * HALF], "h0": [128, NT * D],
               "h0T": [128, NCH * HALF], "h1": [128, NT * D],
               "h1T": [128, NCH * HALF], "f": [128, NT * D]}[t]
        taps[t] = nc.dram_tensor("tap_" + t, shp, BF16,
                                 kind="ExternalOutput").ap()

    ctx = contextlib.ExitStack()
    sbt = lambda name, shape, dt: ctx.enter_context(nc.sbuf_tensor(name, shape, dt))
    A = sbt("A", [128, NCH, HALF], BF16)
    Bt = sbt("Bt", [128, NT, D], BF16)
    KT = sbt("KT", [128, S], BF16)
    Vb = sbt("Vb", [128, 16, 128], BF16)
    QT = sbt("QT", [128, 2, HALF], BF16)
    PTs = sbt("PTs", [128, 4, 512], BF16)
    DR = sbt("DR", [128, 512], F32)
    wst = sbt("wst", [128, 2, 8, 512], BF16)
    xf = sbt("xf", [128, D], F32)
    xfBF = xf.ap().bitcast(BF16)
    ln = sbt("ln", [128, D], BF16)
    oh = sbt("oh", [128, 2048], F32)
    ohBF = oh.ap().bitcast(BF16)
    svec = sbt("svec", [128, 8], F32)
    st = sbt("st", [128, 144], F32)
    idt = sbt("idt", [128, 128], BF16)
    on1 = sbt("on1", [128, 128], BF16)
    onr = sbt("onr", [1, 128], F32)
    tri = sbt("tri_sb", [128, 128], BF16)
    cs = sbt("cs_sb", [128, NCH + 2], F32)
    eb = sbt("eb_sb", [128, EB_TAB.shape[1]], F32)

    pb = [ctx.enter_context(nc.psum_tensor(f"pb{i}", [128, 512], F32))
          for i in range(8)]
    pbBF = pb[7].ap().bitcast(BF16)

    class Sem:
        def __init__(self, name):
            self.sem = ctx.enter_context(nc.semaphore(name))
            self.n = 0

    sX = Sem("sX")
    sW = Sem("sW")
    sW1 = Sem("sW1")
    sPF0 = Sem("sPF0")
    sPF1 = Sem("sPF1")
    sXG = Sem("sXG")
    sXG1 = Sem("sXG1")
    sAF = Sem("sAF")
    sTW = Sem("sTW")
    sEV = Sem("sEV")
    sT = Sem("sT")
    sV = Sem("sV")
    sA = Sem("sA")
    sQr = Sem("sQr")
    sQf = Sem("sQf")
    sST = Sem("sST")
    sMSK = Sem("sMSK")
    sEXP = Sem("sEXP")
    sES = Sem("sES")
    sPV = Sem("sPV")
    sEVq = Sem("sEVq")
    sSP = Sem("sSP")
    sX1 = Sem("sX1")
    sOUT = Sem("sOUT")
    qinst = [0]

    def ln_sums_v(v, src_ap):
        v.reduce_sum(st[:, 0:1], src_ap, axis=AX.X).then_inc(sV.sem, 1)

    def ln_sq_s(s, src_ap, scr=None):
        s.activation(xf[:, :] if scr is None else scr, src_ap, AF.Square,
                     accum_out=st[:, 1:2]).then_inc(sA.sem, 1)

    def ln_var_v(v):
        v.drain()
        v.tensor_scalar(st[:, 2:3], st[:, 0:1], 1.0 / D, None, ALU.mult)
        v.tensor_scalar(st[:, 3:4], st[:, 1:2], 1.0 / D, None, ALU.mult)
        v.drain()
        v.scalar_tensor_tensor(st[:, 4:5], st[:, 2:3], -1.0, st[:, 2:3],
                               ALU.mult, ALU.mult)
        v.drain()
        v.tensor_tensor(st[:, 5:6], st[:, 3:4], st[:, 4:5], ALU.add
                        ).then_inc(sV.sem, 1)

    def ln_s(s):
        s.activation(st[:, 6:7], st[:, 5:6], AF.Ln, bias=st[:, 9:10])
        s.drain()
        s.activation(st[:, 7:8], st[:, 6:7], AF.Exp, scale=st[:, 10:11]
                     ).then_inc(sA.sem, 1)

    def ln_v2(v):
        v.scalar_tensor_tensor(st[:, 8:9], st[:, 2:3], -1.0, st[:, 7:8],
                               ALU.mult, ALU.mult).then_inc(sV.sem, 1)

    def tap_dump(name, src_ap):
        if name not in taps:
            return
        with nc.Block() as blk:
            @blk.sync
            def _(sy):
                sy.dma_start(out=taps[name], in_=src_ap).then_inc(sOUT.sem, 16)
                sOUT.n += 16
                sy.wait_ge(sOUT.sem, sOUT.n)

    with ctx:
        with nc.Block() as blk:
            @blk.sync
            def _(sy):
                for dst, src in ((idt[:], id_h), (on1[:], on_h), (onr[:], onr_h),
                                 (tri[:], tri_h), (cs[:], cs_h), (eb[:], eb_h)):
                    sy.dma_start(out=dst, in_=src).then_inc(sX.sem, 16)
                sy.dma_start(out=st[0:1, 0:4], in_=sv_h).then_inc(sX.sem, 16)
                sX.n += 7 * 16

            @blk.tensor
            def _(t):
                t.wait_ge(sX.sem, sX.n)
                t.matmul(pb[0][:, 0:4], onr[:, :], st[0:1, 0:4],
                         start=True, stop=True).then_inc(sT.sem, 1)
                sT.n += 1

            @blk.vector
            def _(v):
                v.memset(st[:, 9:10], EPS)
                v.memset(st[:, 10:11], -0.5)
                v.memset(st[:, 11:12], -1.0)
                v.wait_ge(sT.sem, sT.n)
                v.tensor_scalar(svec[:, 0:4], pb[0][:, 0:4], 1.0, None, ALU.mult)
                v.drain()
                v.tensor_scalar(svec[:, 4:6], svec[:, 0:2], 2.0, None, ALU.mult)
                v.tensor_scalar(svec[:, 6:7], svec[:, 2:3], 1.0, None, ALU.mult)

        def phase_P0(qh):
            q0 = qh * HALF
            bX, bX1 = sXG.n, sXG1.n
            bV, bA, bT, bE = sV.n, sA.n, sT.n, sEV.n

            def xg_wait(eng, tl):
                sm, b = (sXG, bX) if tl % 2 == 0 else (sXG1, bX1)
                eng.wait_ge(sm.sem, b + 16 * (tl // 2 + 1))
            pb3 = pbBF.rearrange("p (c n) -> p c n", n=128)
            with nc.Block() as blk:
                @blk.gpsimd
                def _(gp):
                    for tl in range(NT):
                        if tl > 1:
                            gp.wait_ge(sA.sem, bA + 3 * (tl - 1))
                        gp.dma_start(
                            out=xfBF[:, (tl % 2) * D:(tl % 2 + 1) * D],
                            in_=x_h[q0 + tl * 128:q0 + (tl + 1) * 128, :]
                        ).then_inc((sXG if tl % 2 == 0 else sXG1).sem, 16)

                @blk.vector
                def _(v):
                    for tl in range(NT):
                        xs = xfBF[:, (tl % 2) * D:(tl % 2 + 1) * D]
                        xg_wait(v, tl)
                        ln_sums_v(v, xs)
                        v.wait_ge(sA.sem, bA + 3 * tl + 1)
                        ln_var_v(v)
                        v.wait_ge(sA.sem, bA + 3 * tl + 2)
                        ln_v2(v)
                        for r in range(4):
                            v.wait_ge(sT.sem, bT + 4 * tl + r + 1)
                            v.tensor_copy(
                                A[:, r * 8:r * 8 + 8, tl * 128:(tl + 1) * 128],
                                pb3[:, 0:8, :]).then_inc(sEV.sem, 1)

                @blk.scalar
                def _(s):
                    for tl in range(NT):
                        xs = xfBF[:, (tl % 2) * D:(tl % 2 + 1) * D]
                        xg_wait(s, tl)
                        if tl == 0 and sOUT.n:
                            s.wait_ge(sOUT.sem, sOUT.n)
                        ln_sq_s(s, xs, ohBF[:, :])
                        if tl > 0:
                            s.wait_ge(sT.sem, bT + 4 * tl)
                        s.wait_ge(sV.sem, bV + 3 * tl + 2)
                        ln_s(s)
                        s.wait_ge(sV.sem, bV + 3 * tl + 3)
                        s.activation(ln[:, :], xs, AF.Identity,
                                     bias=st[:, 8:9], scale=st[:, 7:8]
                                     ).then_inc(sA.sem, 1)

                @blk.tensor
                def _(t):
                    for tl in range(NT):
                        t.wait_ge(sA.sem, bA + 3 * tl + 3)
                        for r in range(4):
                            if tl + r > 0:
                                t.wait_ge(sEV.sem, bE + 4 * tl + r)
                            for c in range(r * 8, r * 8 + 8):
                                i = t.transpose(
                                    pbBF[:, (c % 8) * 128:(c % 8 + 1) * 128],
                                    ln[:, c * 128:(c + 1) * 128], idt[:])
                            i.then_inc(sT.sem, 1)
            sXG.n = bX + 8 * NT
            sXG1.n = bX1 + 8 * NT
            sV.n = bV + 3 * NT
            sA.n = bA + 3 * NT
            sT.n = bT + 4 * NT
            sEV.n = bE + 4 * NT

        def emit_pf(gp, wsrc, widths, tw_end=None):
            if tw_end is None:
                tw_end = sTW.n
            bases = (sPF0.n, sPF1.n, tw_end)
            for n in range(2):
                if tw_end - 1 + n > 0:
                    gp.wait_ge(sTW.sem, tw_end - 1 + n)
                gp.dma_start(out=wst[:, n % 2, :, 0:widths[0]],
                             in_=wsrc(0, n)).then_inc((sPF0, sPF1)[n].sem, 16)
            (sPF0, sPF1)[n].n += 0
            sPF0.n += 16
            sPF1.n += 16
            return bases

        def slice_phase(n_slices, widths, wsrc, lhsT_get, affine,
                        evac_engine, evac_fn, pre_evac_wait=None,
                        pf_in=None, pf_next=None):
            if pf_in is not None:
                bWb = pf_in[:2]
                bTW = pf_in[2]
            else:
                bWb = (sW.n, sW1.n)
                bTW = sTW.n
            sWb = (sW, sW1)
            bAF, bE = sAF.n, sEV.n
            ngrp = 4 * n_slices
            pf_bases = []
            with nc.Block() as blk:
                @blk.gpsimd
                def _(gp):
                    for n in range(0 if pf_in is None else 2, ngrp):
                        if n >= 2:
                            gp.wait_ge(sTW.sem, bTW + n - 1)
                        gp.dma_start(out=wst[:, n % 2, :, 0:widths[n // 4]],
                                     in_=wsrc(n // 4, n % 4)
                                     ).then_inc(sWb[n % 2].sem, 16)
                    if pf_next is not None:
                        pf_bases.append(emit_pf(gp, pf_next[0], pf_next[1],
                                                tw_end=bTW + ngrp))

                if affine:
                    @blk.vector
                    def _(v):
                        for n in range(ngrp):
                            if pf_in is not None and n < 2:
                                v.wait_ge((sPF0, sPF1)[n].sem, pf_in[n] + 16)
                            else:
                                v.wait_ge(sWb[n % 2].sem,
                                          bWb[n % 2] + 16 * (n // 2 + 1))
                            Wd = widths[n // 4]
                            w = wst[:, n % 2]
                            if affine != "cs" or n % 4 < 3:
                                i = v.tensor_scalar(w[:, :, 0:Wd], w[:, :, 0:Wd],
                                                    2.0, -1.0, ALU.mult, ALU.add)
                            else:
                                v.tensor_scalar(w[:, 0:1, 0:Wd], w[:, 0:1, 0:Wd],
                                                2.0, -1.0, ALU.mult, ALU.add)
                                v.tensor_scalar(w[:, 1:2, 0:Wd], w[:, 1:2, 0:Wd],
                                                cs[:, NCH:NCH + 1],
                                                cs[:, NCH + 1:NCH + 2],
                                                ALU.mult, ALU.add)
                                i = v.tensor_scalar(w[:, 2:8, 0:Wd],
                                                    w[:, 2:8, 0:Wd],
                                                    -8.0, 4.0, ALU.mult, ALU.add)
                            i.then_inc(sAF.sem, 1)

                @blk.tensor
                def _(t):
                    for sl in range(n_slices):
                        Wd = widths[sl]
                        for gq in range(4):
                            n = sl * 4 + gq
                            if affine:
                                t.wait_ge(sAF.sem, bAF + n + 1)
                            elif pf_in is not None and n < 2:
                                t.wait_ge((sPF0, sPF1)[n].sem,
                                          pf_in[n] + 16)
                            else:
                                t.wait_ge(sWb[n % 2].sem,
                                          bWb[n % 2] + 16 * (n // 2 + 1))
                            for tl in range(NT):
                                if gq == 0 and sl > 0:
                                    t.wait_ge(sEV.sem,
                                              bE + (sl - 1) * 8 + tl + 1)
                                for ii in range(8):
                                    i = gq * 8 + ii
                                    mi = t.matmul(pb[tl][:, 0:Wd],
                                                  lhsT_get(i, tl),
                                                  wst[:, n % 2, ii, 0:Wd],
                                                  start=(i == 0),
                                                  stop=(i == 31))
                            mi.then_inc(sTW.sem, 1)

                eng_dec = blk.vector if evac_engine == "v" else blk.scalar

                @eng_dec
                def _(e):
                    if pre_evac_wait is not None:
                        pre_evac_wait(e)
                    for sl in range(n_slices):
                        e.wait_ge(sTW.sem, bTW + 4 * (sl + 1))
                        for tl in range(NT):
                            i = evac_fn(e, sl, tl)
                            i.then_inc(sEV.sem, 1)
            sW.n = bWb[0] + 16 * ((ngrp + 1) // 2 - (1 if pf_in is not None else 0))
            sW1.n = bWb[1] + 16 * (ngrp // 2 - (1 if pf_in is not None else 0))
            if affine:
                sAF.n = bAF + ngrp
            sTW.n = bTW + ngrp
            sEV.n = bE + 8 * n_slices
            return pf_bases[0] if pf_bases else None

        def phase_P1(qh, pf_in=None, pf_next=None):
            q0 = qh * HALF
            t0 = qh * NT
            widths = [512] * 8 + [256]

            def wsrc(sl, gq):
                Wd = widths[sl]
                src = wqkv_h[gq * 1024:(gq + 1) * 1024,
                             sl * 512:sl * 512 + Wd]
                return src.rearrange("(c p) n -> p c n", p=128)

            def evac(e, sl, tl):
                if sl < 8:
                    return e.tensor_copy(Bt[:, tl, sl * 512:(sl + 1) * 512],
                                         pb[tl][:, 0:512])
                e.tensor_scalar(ln[:, tl * 128:(tl + 1) * 128],
                                pb[tl][:, 0:128], SCALE, None, ALU.mult)
                return e.tensor_copy(Vb[:, t0 + tl, :], pb[tl][:, 128:256])

            pf = slice_phase(9, widths, wsrc,
                             lambda i, tl: A[:, i, tl * 128:(tl + 1) * 128],
                             affine=False, evac_engine="v", evac_fn=evac,
                             pf_in=pf_in, pf_next=pf_next)

            bT, bE = sT.n, sEV.n
            with nc.Block() as blk:
                @blk.tensor
                def _(t):
                    for tl in range(NT):
                        i = t.transpose(pbBF[:, tl * 128:(tl + 1) * 128],
                                        ln[:, tl * 128:(tl + 1) * 128], idt[:])
                    i.then_inc(sT.sem, 1)

                @blk.vector
                def _(v):
                    v.wait_ge(sT.sem, bT + 1)
                    v.tensor_copy(KT[:, q0:q0 + HALF], pbBF[:, 0:1024]
                                  ).then_inc(sEV.sem, 1)
            sT.n = bT + 1
            sEV.n = bE + 1
            return pf

        def phase_P2(qh, p2_pf=None):
            triples = []
            head_last = {}
            for h in range(NH):
                for qloc in range(2):
                    qq = qh * 2 + qloc
                    for kt in range(_ktmin(h, qq), 4 * qq + 4):
                        triples.append((h, qloc, kt))
                head_last[h] = len(triples) - 1
            ntrip = len(triples)
            diag_upto = []
            dc = 0
            for (h, qloc, kt) in triples:
                if kt >= 4 * (qh * 2 + qloc):
                    dc += 1
                diag_upto.append(dc)
            qi = {}
            for j, (h, qloc, kt) in enumerate(triples):
                if (h, qloc) not in qi:
                    qi[(h, qloc)] = qinst[0] + len(qi)
            q_last_j = {}
            for j, (h, qloc, kt) in enumerate(triples):
                q_last_j[(h, qloc)] = j

            bQr, bQf, bST, bMSK, bEXP, bPV, bT, bES = (
                sQr.n, sQf.n, sST.n, sMSK.n, sEXP.n, sPV.n, sT.n, sES.n)
            qi0 = qinst[0]

            def trip_parts(j):
                h, qloc, kt = triples[j]
                qq = qh * 2 + qloc
                qs, Wd, subs = _subtiles(h, qq, kt)
                n = qi[(h, qloc)]
                return h, qloc, kt, qq, qs, Wd, subs, n

            def emit_dpv(t, j):
                h, qloc, kt, qq, qs, Wd, subs, n = trip_parts(j)
                nkt = 4 * qq + 4
                km = _ktmin(h, qq)
                t.wait_ge(sEXP.sem, bEXP + j + 1)
                di = t.matmul(pb[2 + n % 2][:, qs:512], on1[:, :],
                              PTs[:, j % 4, 0:Wd],
                              start=(kt == km), stop=(kt == nkt - 1))
                if j == head_last[h]:
                    di.then_inc(sQf.sem, 1)
                mi = t.matmul(pb[n % 2][:, qs:512], Vb[:, kt, :],
                              PTs[:, j % 4, 0:Wd],
                              start=(kt == km), stop=(kt == nkt - 1))
                mi.then_inc(sPV.sem, 1)

            with nc.Block() as blk:
                if p2_pf is not None:
                    pf_out2 = []

                    @blk.gpsimd
                    def _(gp):
                        pf_out2.append(emit_pf(gp, p2_pf[0], p2_pf[1]))

                @blk.tensor
                def _(t):
                    j = 0
                    for h in range(NH):
                        if h >= 1:
                            t.wait_ge(sQr.sem, bQr + h)
                        for tl in range(NT):
                            i = t.transpose(
                                pbBF[:, tl * 128:(tl + 1) * 128],
                                Bt[:, tl, h * 128:(h + 1) * 128], idt[:])
                        i.then_inc(sT.sem, 1)
                        jh0 = j
                        for qloc in range(2):
                            qq = qh * 2 + qloc
                            nkt = 4 * qq + 4
                            for kt in range(_ktmin(h, qq), nkt):
                                _, _, _, _, qs, Wd, subs, n = trip_parts(j)
                                if j >= 3:
                                    t.wait_ge(sEXP.sem, bEXP + j - 2)
                                if qloc == 0 and kt == _ktmin(h, qq):
                                    t.wait_ge(sQr.sem, bQr + h + 1)
                                if kt == _ktmin(h, qq) and n >= 2:
                                    t.wait_ge(sEVq.sem, 2 * (n - 1))
                                diag = kt >= 4 * qq
                                mi = t.matmul(
                                    pb[4 + j % 3][:, 0:Wd],
                                    KT[:, kt * 128:(kt + 1) * 128],
                                    QT[:, h % 2,
                                       qloc * 512 + qs:qloc * 512 + 512],
                                    start=True, stop=not diag)
                                if diag:
                                    mi = t.matmul(pb[4 + j % 3][:, 0:128],
                                                  idt[:], tri[:],
                                                  start=False, stop=True)
                                mi.then_inc(sST.sem, 1)
                                if j > jh0 + 1:
                                    emit_dpv(t, j - 2)
                                j += 1
                        if j > jh0 + 1:
                            emit_dpv(t, j - 2)
                        emit_dpv(t, j - 1)

                @blk.vector
                def _(v):
                    def evac(hh, qloc):
                        n = qi[(hh, qloc)]
                        v.wait_ge(sPV.sem, bPV + q_last_j[(hh, qloc)] + 1)
                        if n >= 1:
                            v.wait_ge(sEVq.sem, 2 * n)
                        v.reciprocal(DR[:, :], pb[2 + n % 2][:, :]
                                     ).then_inc(sEVq.sem, 1)
                        v.wait_ge(sEVq.sem, 2 * n + 1)
                        v.tensor_tensor(
                            A[:, hh, qloc * 512:(qloc + 1) * 512],
                            pb[n % 2][:, :], DR[:, :], ALU.mult
                        ).then_inc(sEVq.sem, 1)

                    for h in range(NH):
                        v.wait_ge(sT.sem, bT + h + 1)
                        if h >= 2:
                            v.wait_ge(sQf.sem, bQf + h - 1)
                        v.tensor_copy(QT[:, h % 2, :], pbBF[:, 0:1024]
                                      ).then_inc(sQr.sem, 1)
                        if h >= 1:
                            evac(h - 1, 1)
                        evac(h, 0)
                    evac(NH - 1, 1)

                @blk.scalar
                def _(s):
                    j = 0
                    for h in range(NH):
                        for qloc in range(2):
                            qq = qh * 2 + qloc
                            nkt = 4 * qq + 4
                            for kt in range(_ktmin(h, qq), nkt):
                                _, _, _, _, qs, Wd, subs, n = trip_parts(j)
                                s.wait_ge(sST.sem, bST + j + 1)
                                if j >= 4:
                                    s.wait_ge(sPV.sem, bPV + j - 3)
                                for (off, w, delta) in subs:
                                    col = EB_COLS[(h, delta)]
                                    i = s.activation(
                                        PTs[:, j % 4, off:off + w],
                                        pb[4 + j % 3][:, off:off + w],
                                        AF.Exp, bias=eb[:, col:col + 1])
                                i.then_inc(sEXP.sem, 1)
                                j += 1

            sQr.n = bQr + NH
            sQf.n = bQf + NH
            sST.n = bST + ntrip
            sEXP.n = bEXP + ntrip
            sPV.n = bPV + ntrip
            sT.n = bT + NH
            qinst[0] += 2 * NH
            sEVq.n = 2 * qinst[0]
            return pf_out2[0] if p2_pf is not None else None

        def phase_P3(qh, pf_in=None, pf_next=None):
            widths = [512] * 8

            def wsrc(sl, gq):
                src = wo_h[gq * 1024:(gq + 1) * 1024,
                           sl * 512:(sl + 1) * 512]
                return src.rearrange("(c p) n -> p c n", p=128)

            def evac(e, sl, tl):
                e.activation(Bt[:, tl, sl * 512:(sl + 1) * 512],
                             pb[tl][:, 0:512], AF.Identity,
                             accum_out=st[:, 16 + 8 * tl + sl:
                                          17 + 8 * tl + sl])
                return e.activation(oh[:, 0:512], pb[tl][:, 0:512], AF.Square,
                                    accum_out=st[:, 80 + 8 * tl + sl:
                                                 81 + 8 * tl + sl])

            return slice_phase(8, widths, wsrc,
                               lambda i, tl: A[:, i, tl * 128:(tl + 1) * 128],
                               affine=False, evac_engine="s", evac_fn=evac,
                               pf_in=pf_in, pf_next=pf_next)

        def phase_P35(qh, p35_pf=None):
            q0 = qh * HALF
            bX, bX1 = sXG.n, sXG1.n
            bV, bA, bT, bE, bSP = sV.n, sA.n, sT.n, sEV.n, sSP.n

            def xg_wait(eng, tl):
                sm, b = (sXG, bX) if tl % 2 == 0 else (sXG1, bX1)
                eng.wait_ge(sm.sem, b + 16 * (tl // 2 + 1))
            pb3 = pbBF.rearrange("p (c n) -> p c n", n=128)
            pf_out = []
            with nc.Block() as blk:
                @blk.gpsimd
                def _(gp):
                    if p35_pf is not None:
                        pf_out.append(emit_pf(gp, p35_pf[0], p35_pf[1]))
                    for tl in range(NT):
                        if tl > 1:
                            gp.wait_ge(sV.sem, bV + 7 * (tl - 2) + 4)
                        gp.dma_start(
                            out=xfBF[:, (tl % 2) * D:(tl % 2 + 1) * D],
                            in_=x_h[q0 + tl * 128:q0 + (tl + 1) * 128, :]
                        ).then_inc((sXG if tl % 2 == 0 else sXG1).sem, 16)

                @blk.sync
                def _(sy):
                    for tl in range(NT):
                        sy.wait_ge(sV.sem, bV + 7 * tl + 4)
                        sy.dma_start(
                            out=x1sp[q0 + tl * 128:q0 + (tl + 1) * 128, :],
                            in_=Bt[:, tl, :]).then_inc(sSP.sem, 16)

                @blk.vector
                def _(v):
                    for tl in range(NT):
                        v.reduce_sum(st[:, 0:1],
                                     st[:, 16 + 8 * tl:24 + 8 * tl],
                                     axis=AX.X)
                        v.reduce_sum(st[:, 1:2],
                                     st[:, 80 + 8 * tl:88 + 8 * tl],
                                     axis=AX.X).then_inc(sV.sem, 1)
                        ln_var_v(v)
                        v.wait_ge(sA.sem, bA + 5 * tl + 1)
                        ln_v2(v)
                        v.wait_ge(sA.sem, bA + 5 * tl + 2)
                        xg_wait(v, tl)
                        v.tensor_tensor(Bt[:, tl, :],
                                        xfBF[:, (tl % 2) * D:(tl % 2 + 1) * D],
                                        ln[:, :], ALU.add
                                        ).then_inc(sV.sem, 1)
                        v.drain()
                        ln_sums_v(v, Bt[:, tl, :])
                        v.wait_ge(sA.sem, bA + 5 * tl + 3)
                        ln_var_v(v)
                        v.wait_ge(sA.sem, bA + 5 * tl + 4)
                        ln_v2(v)
                        for r in range(4):
                            v.wait_ge(sT.sem, bT + 4 * tl + r + 1)
                            v.tensor_copy(
                                A[:, r * 8:r * 8 + 8, tl * 128:(tl + 1) * 128],
                                pb3[:, 0:8, :]).then_inc(sEV.sem, 1)

                @blk.scalar
                def _(s):
                    for tl in range(NT):
                        if tl > 0:
                            s.wait_ge(sT.sem, bT + 4 * tl)
                        s.wait_ge(sV.sem, bV + 7 * tl + 2)
                        ln_s(s)
                        s.wait_ge(sV.sem, bV + 7 * tl + 3)
                        s.activation(ln[:, :], Bt[:, tl, :], AF.Identity,
                                     bias=st[:, 8:9], scale=st[:, 7:8]
                                     ).then_inc(sA.sem, 1)
                        s.wait_ge(sV.sem, bV + 7 * tl + 4)
                        if tl == 0 and sOUT.n:
                            s.wait_ge(sOUT.sem, sOUT.n)
                        ln_sq_s(s, Bt[:, tl, :], ohBF[:, :])
                        s.wait_ge(sV.sem, bV + 7 * tl + 6)
                        ln_s(s)
                        s.wait_ge(sV.sem, bV + 7 * tl + 7)
                        s.activation(ln[:, :], Bt[:, tl, :], AF.Identity,
                                     bias=st[:, 8:9], scale=st[:, 7:8]
                                     ).then_inc(sA.sem, 1)

                @blk.tensor
                def _(t):
                    for tl in range(NT):
                        t.wait_ge(sA.sem, bA + 5 * tl + 5)
                        for r in range(4):
                            if tl + r > 0:
                                t.wait_ge(sEV.sem, bE + 4 * tl + r)
                            for c in range(r * 8, r * 8 + 8):
                                i = t.transpose(
                                    pbBF[:, (c % 8) * 128:(c % 8 + 1) * 128],
                                    ln[:, c * 128:(c + 1) * 128], idt[:])
                            i.then_inc(sT.sem, 1)
            sXG.n = bX + 8 * NT
            sXG1.n = bX1 + 8 * NT
            sV.n = bV + 7 * NT
            sA.n = bA + 5 * NT
            sT.n = bT + 4 * NT
            sEV.n = bE + 4 * NT
            sSP.n = bSP + 16 * NT

        def arena_chunk(arena, c, w=HALF):
            if arena is A:
                return A[:, c, 0:w]
            return Bt[:, c // 4, (c % 4) * HALF:(c % 4) * HALF + w]

        def ffn_fm_phase(qh, layer, src_arena, dst_arena, colscale,
                         pre_evac_wait=None, pf_in=None, pf_next=None):
            kh = k_h[layer]
            scol = (4, 5, 6)[layer]
            func = AF.Relu if layer < 2 else AF.Identity
            if pf_in is not None:
                bWb = pf_in[:2]
                bTW = pf_in[2]
            else:
                bWb = (sW.n, sW1.n)
                bTW = sTW.n
            sWb = (sW, sW1)
            bAF, bE = sAF.n, sEV.n
            npair = 16
            ngrp = 4 * npair
            pf_bases = []

            def wsrc(p, g):
                return kh[g * 1024:(g + 1) * 1024,
                          p * 256:(p + 1) * 256
                          ].rearrange("(c p) n -> p c n", p=128)

            with nc.Block() as blk:
                @blk.gpsimd
                def _(gp):
                    for n in range(0 if pf_in is None else 2, ngrp):
                        if n >= 2:
                            gp.wait_ge(sTW.sem, bTW + n - 1)
                        gp.dma_start(out=wst[:, n % 2, :, 0:256],
                                     in_=wsrc(n // 4, n % 4)
                                     ).then_inc(sWb[n % 2].sem, 16)
                    if pf_next is not None:
                        pf_bases.append(emit_pf(gp, pf_next[0], pf_next[1],
                                                tw_end=bTW + ngrp))

                @blk.vector
                def _(v):
                    for n in range(ngrp):
                        if pf_in is not None and n < 2:
                            v.wait_ge((sPF0, sPF1)[n].sem, pf_in[n] + 16)
                        else:
                            v.wait_ge(sWb[n % 2].sem,
                                      bWb[n % 2] + 16 * (n // 2 + 1))
                        g = n % 4
                        w = wst[:, n % 2]
                        if (not colscale) or g < 3:
                            i = v.tensor_scalar(w[:, :, 0:256], w[:, :, 0:256],
                                                2.0, -1.0, ALU.mult, ALU.add)
                        else:
                            v.tensor_scalar(w[:, 0:1, 0:256], w[:, 0:1, 0:256],
                                            2.0, -1.0, ALU.mult, ALU.add)
                            v.tensor_scalar(w[:, 1:2, 0:256], w[:, 1:2, 0:256],
                                            cs[:, NCH:NCH + 1],
                                            cs[:, NCH + 1:NCH + 2],
                                            ALU.mult, ALU.add)
                            i = v.tensor_scalar(w[:, 2:8, 0:256],
                                                w[:, 2:8, 0:256],
                                                -8.0, 4.0, ALU.mult, ALU.add)
                        i.then_inc(sAF.sem, 1)

                @blk.tensor
                def _(t):
                    for p in range(npair):
                        pbase = 4 * (p % 2)
                        for g in range(4):
                            n = p * 4 + g
                            t.wait_ge(sAF.sem, bAF + n + 1)
                            if g == 0 and p >= 2:
                                t.wait_ge(sEV.sem, bE + 2 * (p - 1))
                            for ii in range(8):
                                i = g * 8 + ii
                                for oc in range(2):
                                    for sh in range(2):
                                        mi = t.matmul(
                                            pb[pbase + 2 * oc + sh][:, 0:512],
                                            wst[:, n % 2, ii,
                                                oc * 128:oc * 128 + 128],
                                            arena_chunk(src_arena, i
                                                        )[:, sh * 512:
                                                          sh * 512 + 512],
                                            start=(i == 0), stop=(i == 31))
                            mi.then_inc(sTW.sem, 1)

                @blk.scalar
                def _(s):
                    if pre_evac_wait is not None:
                        pre_evac_wait(s)
                    for p in range(npair):
                        pbase = 4 * (p % 2)
                        s.wait_ge(sTW.sem, bTW + 4 * (p + 1))
                        for oc in range(2):
                            dst = arena_chunk(dst_arena, 2 * p + oc)
                            ps2 = pb[pbase + 2 * oc].ap()
                            s.activation(dst[:, 0:512], ps2[:, 0:512], func,
                                         scale=svec[:, scol:scol + 1])
                            s.activation(dst[:, 512:1024],
                                         pb[pbase + 2 * oc + 1][:, 0:512],
                                         func, scale=svec[:, scol:scol + 1]
                                         ).then_inc(sEV.sem, 1)

            sW.n = bWb[0] + 16 * ((ngrp + 1) // 2 - (1 if pf_in is not None else 0))
            sW1.n = bWb[1] + 16 * (ngrp // 2 - (1 if pf_in is not None else 0))
            sAF.n = bAF + ngrp
            sTW.n = bTW + ngrp
            sEV.n = bE + 2 * npair
            return pf_bases[0] if pf_bases else None

        def p4_wsrc(layer):
            def wsrc(sl, gq):
                src = k_h[layer][gq * 1024:(gq + 1) * 1024,
                                 sl * 512:(sl + 1) * 512]
                return src.rearrange("(c p) n -> p c n", p=128)
            return wsrc

        def phase_P4c(qh, pf_in=None, pf_next=None):
            def evac(e, sl, tl):
                return e.activation(Bt[:, tl, sl * 512:(sl + 1) * 512],
                                    pb[tl][:, 0:512], AF.Identity,
                                    scale=svec[:, 6:7])

            return slice_phase(8, [512] * 8, p4_wsrc(2),
                               lambda i, tl: A[:, i, tl * 128:(tl + 1) * 128],
                               affine="cs", evac_engine="s", evac_fn=evac,
                               pf_in=pf_in, pf_next=pf_next)

        def phase_P4T(qh):
            bT, bE = sT.n, sEV.n
            pb3 = pbBF.rearrange("p (c n) -> p c n", n=128)
            with nc.Block() as blk:
                @blk.tensor
                def _(t):
                    for tl in range(NT):
                        for r in range(4):
                            if tl + r > 0:
                                t.wait_ge(sEV.sem, bE + 4 * tl + r)
                            for c in range(r * 8, r * 8 + 8):
                                i = t.transpose(
                                    pbBF[:, (c % 8) * 128:(c % 8 + 1) * 128],
                                    Bt[:, tl, c * 128:(c + 1) * 128], idt[:])
                            i.then_inc(sT.sem, 1)

                @blk.vector
                def _(v):
                    for tl in range(NT):
                        dst = lambda lo, hi: A[:, lo:hi, tl * 128:(tl + 1) * 128]
                        for r in range(4):
                            v.wait_ge(sT.sem, bT + 4 * tl + r + 1)
                            if r < 3:
                                i = v.tensor_copy(dst(r * 8, r * 8 + 8),
                                                  pb3[:, 0:8, :])
                            else:
                                v.tensor_copy(dst(24, 25), pb3[:, 0:1, :])
                                v.tensor_scalar(dst(25, 26), pb3[:, 1:2, :],
                                                cs[:, 25:26], None, ALU.mult)
                                i = v.tensor_scalar(dst(26, 32), pb3[:, 2:8, :],
                                                    EI, None, ALU.mult)
                            i.then_inc(sEV.sem, 1)
            sT.n = bT + 4 * NT
            sEV.n = bE + 4 * NT

        def phase_P5(qh, p5_pf=None):
            q0 = qh * HALF
            bV, bA, bX1, bO = sV.n, sA.n, sX1.n, sOUT.n
            pf_out = []
            with nc.Block() as blk:
                if p5_pf is not None:
                    @blk.gpsimd
                    def _(gp):
                        pf_out.append(emit_pf(gp, p5_pf[0], p5_pf[1]))
                @blk.sync
                def _(sy):
                    sy.wait_ge(sSP.sem, sSP.n)
                    for tl in range(NT):
                        row = q0 + tl * 128
                        if tl > 0:
                            sy.wait_ge(sV.sem, bV + 5 * tl)
                        sy.dma_start(
                            out=ln[:, 2048:4096],
                            in_=x1sp[row:row + 128, 0:2048]
                        ).then_inc(sX1.sem, 16)
                        sy.wait_ge(sV.sem, bV + 5 * tl + 4)
                        sy.dma_start(
                            out=ln[:, 2048:4096],
                            in_=x1sp[row:row + 128, 2048:4096]
                        ).then_inc(sX1.sem, 16)
                        sy.dma_start(
                            out=out_h[row:row + 128, 0:2048],
                            in_=oh[:, :]).then_inc(sOUT.sem, 16)
                        sy.wait_ge(sV.sem, bV + 5 * tl + 5)
                        sy.dma_start(
                            out=out_h[row:row + 128, 2048:4096],
                            in_=oh[:, :]).then_inc(sOUT.sem, 16)

                @blk.vector
                def _(v):
                    for tl in range(NT):
                        if tl > 0:
                            v.drain()
                        ln_sums_v(v, Bt[:, tl, :])
                        v.wait_ge(sA.sem, bA + 4 * tl + 1)
                        ln_var_v(v)
                        v.wait_ge(sA.sem, bA + 4 * tl + 2)
                        ln_v2(v)
                        for jj in range(2):
                            v.wait_ge(sX1.sem, bX1 + 16 * (2 * tl + jj + 1))
                            v.wait_ge(sA.sem, bA + 4 * tl + 3 + jj)
                            if jj == 1:
                                v.wait_ge(sOUT.sem, bO + 32 * tl + 16)
                            else:
                                v.wait_ge(sOUT.sem, bO + 32 * tl)
                            v.tensor_tensor(oh[:, :], ln[:, 2048:4096],
                                            ln[:, 0:2048], ALU.add
                                            ).then_inc(sV.sem, 1)

                @blk.scalar
                def _(s):
                    for tl in range(NT):
                        if tl > 0:
                            s.wait_ge(sV.sem, bV + 5 * tl)
                        ln_sq_s(s, Bt[:, tl, :])
                        s.wait_ge(sV.sem, bV + 5 * tl + 2)
                        ln_s(s)
                        s.wait_ge(sV.sem, bV + 5 * tl + 3)
                        s.activation(ln[:, 0:2048], Bt[:, tl, 0:2048],
                                     AF.Identity, bias=st[:, 8:9],
                                     scale=st[:, 7:8]).then_inc(sA.sem, 1)
                        s.wait_ge(sV.sem, bV + 5 * tl + 4)
                        s.activation(ln[:, 0:2048], Bt[:, tl, 2048:4096],
                                     AF.Identity, bias=st[:, 8:9],
                                     scale=st[:, 7:8]).then_inc(sA.sem, 1)
            sV.n = bV + 5 * NT
            sA.n = bA + 4 * NT
            sX1.n = bX1 + 32 * NT
            sOUT.n = bO + 32 * NT
            return pf_out[0] if p5_pf is not None else None

        P = _PHASES
        p1_w = (lambda sl, gq: wqkv_h[gq * 1024:(gq + 1) * 1024,
                                      sl * 512:sl * 512 + ([512] * 8 + [256])[sl]
                                      ].rearrange("(c p) n -> p c n", p=128),
                [512] * 8 + [256])
        p3_w = (lambda sl, gq: wo_h[gq * 1024:(gq + 1) * 1024,
                                    sl * 512:(sl + 1) * 512
                                    ].rearrange("(c p) n -> p c n", p=128),
                [512] * 8)
        p4a_w = (lambda p, g: k_h[0][g * 1024:(g + 1) * 1024,
                                     p * 256:(p + 1) * 256
                                     ].rearrange("(c p) n -> p c n", p=128),
                 [256] * 16)
        p4b_w = (lambda p, g: k_h[1][g * 1024:(g + 1) * 1024,
                                     p * 256:(p + 1) * 256
                                     ].rearrange("(c p) n -> p c n", p=128),
                 [256] * 16)
        p4c_w = (lambda sl, gq: k_h[2][gq * 1024:(gq + 1) * 1024,
                                       sl * 512:(sl + 1) * 512
                                       ].rearrange("(c p) n -> p c n", p=128),
                 [512] * 8)

        pf1 = [None]
        if _PF:
            with nc.Block() as blk:
                @blk.gpsimd
                def _(gp):
                    pf1[0] = emit_pf(gp, p1_w[0], p1_w[1])

        pf = pf1[0]
        for qh in range(_HALVES):
            phase_P0(qh)
            if qh == 0:
                tap_dump("xnT", A[:, :, :].rearrange("p c n -> p (c n)"))
            pf = phase_P1(qh, pf_in=pf, pf_next=None)
            if qh == 0:
                tap_dump("q", Bt[:, :, :].rearrange("p t n -> p (t n)"))
                tap_dump("kt", KT[:, :])
                tap_dump("v", Vb[:, :, :].rearrange("p t n -> p (t n)"))
            pf = phase_P2(qh, p2_pf=(p3_w if _PF else None))
            if qh == 0:
                tap_dump("attnT", A[:, :, :].rearrange("p c n -> p (c n)"))
            phase_P3(qh, pf_in=pf, pf_next=None)
            if qh == 0:
                tap_dump("y", Bt[:, :, :].rearrange("p t n -> p (t n)"))
            pf = phase_P35(qh, p35_pf=(p4a_w if _PF else None))
            if qh == 0:
                tap_dump("x1", Bt[:, :, :].rearrange("p t n -> p (t n)"))
                tap_dump("znT", A[:, :, :].rearrange("p c n -> p (c n)"))
            spn = sSP.n

            def spw(e, spn=spn):
                e.wait_ge(sSP.sem, spn)

            pf = ffn_fm_phase(qh, 0, A, Bt, colscale=False,
                              pre_evac_wait=spw, pf_in=pf,
                              pf_next=(p4b_w if _PF else None))
            if qh == 0:
                tap_dump("h0T", Bt[:, :, :].rearrange("p t n -> p (t n)"))
            pf = ffn_fm_phase(qh, 1, Bt, A, colscale=True,
                              pf_in=pf, pf_next=(p4c_w if _PF else None))
            if qh == 0:
                tap_dump("h1T", A[:, :, :].rearrange("p c n -> p (c n)"))
            phase_P4c(qh, pf_in=pf, pf_next=None)
            if qh == 0:
                tap_dump("f", Bt[:, :, :].rearrange("p t n -> p (t n)"))
            pf = phase_P5(qh, p5_pf=(p1_w if qh == 0 and _PF else None))

        with nc.Block() as blk:
            @blk.gpsimd
            def _(gp):
                gp.wait_ge(sOUT.sem, sOUT.n)
    return nc


def _consts():
    ident = np.eye(128, dtype=ml_dtypes.bfloat16)
    ones128 = np.ones((128, 128), dtype=ml_dtypes.bfloat16)
    onesrow = np.ones((1, 128), dtype=np.float32)
    pp = np.arange(128)
    tri = np.where(pp[:, None] > pp[None, :], NEG, 0.0).astype(ml_dtypes.bfloat16)
    f = (np.arange(NCH)[None, :] * 128 + pp[:, None])
    cs = np.where(f < NE, np.float32(1.0), np.float32(EI)).astype(np.float32)
    cs = np.concatenate([cs, 2.0 * cs[:, 25:26], -cs[:, 25:26]],
                        axis=1).astype(np.float32)
    return ident, ones128, onesrow, tri, cs


def kernel(L0_attn_hidden_states, L0_attn_Wqkv, L0_attn_Wo,
           L0_ffn_k0, L0_ffn_s0, L0_ffn_k1, L0_ffn_s1, L0_ffn_k2, L0_ffn_s2):
    x = np.ascontiguousarray(np.asarray(L0_attn_hidden_states, dtype=np.float32))
    wqkv = np.ascontiguousarray(np.asarray(L0_attn_Wqkv, dtype=np.float32))
    wo = np.ascontiguousarray(np.asarray(L0_attn_Wo, dtype=np.float32))
    ks = [np.ascontiguousarray(np.asarray(k).view(np.uint8))
          for k in (L0_ffn_k0, L0_ffn_k1, L0_ffn_k2)]
    sv = np.zeros((1, 4), dtype=np.float32)
    sv[0, 0] = np.float32(L0_ffn_s0)
    sv[0, 1] = np.float32(L0_ffn_s1)
    sv[0, 2] = np.float32(L0_ffn_s2)
    ident, ones128, onesrow, tri, cs = _consts()

    nc = build_program()
    base = dict(wqkv=wqkv, wo=wo, k0=ks[0], k1=ks[1], k2=ks[2], svals=sv,
                ident=ident, ones128=ones128, onesrow=onesrow, tri=tri,
                cs=cs, ebias=EB_TAB)
    in_maps = [dict(base, x=np.ascontiguousarray(x[c])) for c in range(8)]
    res = run_bass_kernel_spmd(nc, in_maps, core_ids=list(range(8)))
    out = np.stack([r["out"] for r in res.results], axis=0)
    return out.astype(np.float32)


# version 21
# speedup vs baseline: 1.0111x; 1.0111x over previous
import math
import contextlib
import numpy as np
import ml_dtypes
import concourse.bass as bass
import concourse.mybir as mybir
from concourse.bass_utils import run_bass_kernel_spmd

F32 = mybir.dt.float32
BF16 = mybir.dt.bfloat16
U8 = mybir.dt.uint8
AF = mybir.ActivationFunctionType
ALU = mybir.AluOpType
AX = mybir.AxisListType

D = 4096
S = 2048
NH = 32
HD = 128
QKVC = 4352
NE = 3277
HALF = 1024
NT = 8
NCH = 32
EPS = 1e-5
SCALE = float(np.float32(HD ** -0.5))
NEG = -1e30
EI = -4.0

_TAPS = ()
_PHASES = ("P0", "P1", "P2", "P3", "P35", "P4a", "P4aT", "P4b", "P4bT", "P4c", "P5")
_HALVES = 2
_PF = False


def _slopes():
    base = 2.0 ** (-2.0 ** (-(math.log2(32) - 3)))
    return [float(np.float32(base ** (h + 1))) for h in range(NH)]


SLOPES = _slopes()


def _tier(h):
    if h < 4:
        return 64
    if h < 8:
        return 128
    if h < 12:
        return 256
    return 512


ATHR = 1e18


def _ktmin(h, qq):
    lo = (qq * 512 - 127 - ATHR / SLOPES[h]) / 128.0
    return max(0, int(math.ceil(lo)))


def _subtiles(h, qq, kt):
    qs = max(0, kt * 128 - qq * 512)
    W = 512 - qs
    T = _tier(h)
    qg0 = qq * 512 + qs
    qg_end = (qq + 1) * 512
    subs = []
    q = qg0
    while q < qg_end:
        q_hi = min(qg_end, (q // T + 1) * T)
        subs.append((q - qg0, q_hi - q, kt * 128 - (q_hi - 1)))
        q = q_hi
    return qs, W, subs


def _plan_ebias():
    cols = {}
    for h in range(NH):
        for qq in range(4):
            for kt in range(_ktmin(h, qq), 4 * qq + 4):
                for _, _, delta in _subtiles(h, qq, kt)[2]:
                    cols.setdefault((h, delta), len(cols))
    tab = np.zeros((128, max(1, len(cols))), dtype=np.float32)
    p = np.arange(128, dtype=np.float64)
    for (h, delta), c in cols.items():
        tab[:, c] = (SLOPES[h] * (p + delta)).astype(np.float32)
    return cols, tab


EB_COLS, EB_TAB = _plan_ebias()


def build_program():
    nc = bass.Bass("TRN2", target_bir_lowering=False)

    x_h = nc.dram_tensor("x", [S, D], F32, kind="ExternalInput").ap()
    wqkv_h = nc.dram_tensor("wqkv", [D, QKVC], F32, kind="ExternalInput").ap()
    wo_h = nc.dram_tensor("wo", [D, D], F32, kind="ExternalInput").ap()
    k_h = [nc.dram_tensor(f"k{i}", [D, D], U8, kind="ExternalInput").ap()
           for i in range(3)]
    sv_h = nc.dram_tensor("svals", [1, 4], F32, kind="ExternalInput").ap()
    id_h = nc.dram_tensor("ident", [128, 128], BF16, kind="ExternalInput").ap()
    on_h = nc.dram_tensor("ones128", [128, 128], BF16, kind="ExternalInput").ap()
    onr_h = nc.dram_tensor("onesrow", [1, 128], F32, kind="ExternalInput").ap()
    tri_h = nc.dram_tensor("tri", [128, 128], BF16, kind="ExternalInput").ap()
    cs_h = nc.dram_tensor("cs", [128, NCH + 2], F32, kind="ExternalInput").ap()
    eb_h = nc.dram_tensor("ebias", [128, EB_TAB.shape[1]], F32,
                          kind="ExternalInput").ap()
    x1sp = nc.dram_tensor("x1sp", [S, D], BF16, kind="Internal").ap()
    out_h = nc.dram_tensor("out", [S, D], F32, kind="ExternalOutput").ap()

    taps = {}
    for t in _TAPS:
        shp = {"xnT": [128, NCH * HALF], "q": [128, NT * D], "kt": [128, S],
               "v": [128, 16 * 128], "attnT": [128, NCH * HALF],
               "y": [128, NT * D], "x1": [128, NT * D],
               "znT": [128, NCH * HALF], "h0": [128, NT * D],
               "h0T": [128, NCH * HALF], "h1": [128, NT * D],
               "h1T": [128, NCH * HALF], "f": [128, NT * D]}[t]
        taps[t] = nc.dram_tensor("tap_" + t, shp, BF16,
                                 kind="ExternalOutput").ap()

    ctx = contextlib.ExitStack()
    sbt = lambda name, shape, dt: ctx.enter_context(nc.sbuf_tensor(name, shape, dt))
    A = sbt("A", [128, NCH, HALF], BF16)
    Bt = sbt("Bt", [128, NT, D], BF16)
    KT = sbt("KT", [128, S], BF16)
    Vb = sbt("Vb", [128, 16, 128], BF16)
    QT = sbt("QT", [128, 2, HALF], BF16)
    PTs = sbt("PTs", [128, 4, 512], BF16)
    DR = sbt("DR", [128, 512], F32)
    wst = sbt("wst", [128, 2, 8, 512], BF16)
    xf = sbt("xf", [128, D], F32)
    xfBF = xf.ap().bitcast(BF16)
    ln = sbt("ln", [128, D], BF16)
    oh = sbt("oh", [128, 2048], F32)
    ohBF = oh.ap().bitcast(BF16)
    svec = sbt("svec", [128, 8], F32)
    st = sbt("st", [128, 144], F32)
    idt = sbt("idt", [128, 128], BF16)
    on1 = sbt("on1", [128, 128], BF16)
    onr = sbt("onr", [1, 128], F32)
    tri = sbt("tri_sb", [128, 128], BF16)
    cs = sbt("cs_sb", [128, NCH + 2], F32)
    eb = sbt("eb_sb", [128, EB_TAB.shape[1]], F32)

    pb = [ctx.enter_context(nc.psum_tensor(f"pb{i}", [128, 512], F32))
          for i in range(8)]
    pbBF = pb[7].ap().bitcast(BF16)

    class Sem:
        def __init__(self, name):
            self.sem = ctx.enter_context(nc.semaphore(name))
            self.n = 0

    sX = Sem("sX")
    sW = Sem("sW")
    sW1 = Sem("sW1")
    sPF0 = Sem("sPF0")
    sPF1 = Sem("sPF1")
    sXG = Sem("sXG")
    sXG1 = Sem("sXG1")
    sAF = Sem("sAF")
    sTW = Sem("sTW")
    sEV = Sem("sEV")
    sT = Sem("sT")
    sV = Sem("sV")
    sA = Sem("sA")
    sQr = Sem("sQr")
    sQf = Sem("sQf")
    sST = Sem("sST")
    sMSK = Sem("sMSK")
    sEXP = Sem("sEXP")
    sES = Sem("sES")
    sPV = Sem("sPV")
    sEVq = Sem("sEVq")
    sSP = Sem("sSP")
    sX1 = Sem("sX1")
    sOUT = Sem("sOUT")
    qinst = [0]

    def ln_sums_v(v, src_ap):
        v.reduce_sum(st[:, 0:1], src_ap, axis=AX.X).then_inc(sV.sem, 1)

    def ln_sq_s(s, src_ap, scr=None):
        s.activation(xf[:, :] if scr is None else scr, src_ap, AF.Square,
                     accum_out=st[:, 1:2]).then_inc(sA.sem, 1)

    def ln_var_v(v):
        v.drain()
        v.tensor_scalar(st[:, 2:3], st[:, 0:1], 1.0 / D, None, ALU.mult)
        v.tensor_scalar(st[:, 3:4], st[:, 1:2], 1.0 / D, None, ALU.mult)
        v.drain()
        v.scalar_tensor_tensor(st[:, 4:5], st[:, 2:3], -1.0, st[:, 2:3],
                               ALU.mult, ALU.mult)
        v.drain()
        v.tensor_tensor(st[:, 5:6], st[:, 3:4], st[:, 4:5], ALU.add
                        ).then_inc(sV.sem, 1)

    def ln_s(s):
        s.activation(st[:, 6:7], st[:, 5:6], AF.Ln, bias=st[:, 9:10])
        s.drain()
        s.activation(st[:, 7:8], st[:, 6:7], AF.Exp, scale=st[:, 10:11]
                     ).then_inc(sA.sem, 1)

    def ln_v2(v):
        v.scalar_tensor_tensor(st[:, 8:9], st[:, 2:3], -1.0, st[:, 7:8],
                               ALU.mult, ALU.mult).then_inc(sV.sem, 1)

    def tap_dump(name, src_ap):
        if name not in taps:
            return
        with nc.Block() as blk:
            @blk.sync
            def _(sy):
                sy.dma_start(out=taps[name], in_=src_ap).then_inc(sOUT.sem, 16)
                sOUT.n += 16
                sy.wait_ge(sOUT.sem, sOUT.n)

    with ctx:
        with nc.Block() as blk:
            @blk.sync
            def _(sy):
                for dst, src in ((idt[:], id_h), (on1[:], on_h), (onr[:], onr_h),
                                 (tri[:], tri_h), (cs[:], cs_h), (eb[:], eb_h)):
                    sy.dma_start(out=dst, in_=src).then_inc(sX.sem, 16)
                sy.dma_start(out=st[0:1, 0:4], in_=sv_h).then_inc(sX.sem, 16)
                sX.n += 7 * 16

            @blk.tensor
            def _(t):
                t.wait_ge(sX.sem, sX.n)
                t.matmul(pb[0][:, 0:4], onr[:, :], st[0:1, 0:4],
                         start=True, stop=True).then_inc(sT.sem, 1)
                sT.n += 1

            @blk.vector
            def _(v):
                v.memset(st[:, 9:10], EPS)
                v.memset(st[:, 10:11], -0.5)
                v.memset(st[:, 11:12], -1.0)
                v.wait_ge(sT.sem, sT.n)
                v.tensor_scalar(svec[:, 0:4], pb[0][:, 0:4], 1.0, None, ALU.mult)
                v.drain()
                v.tensor_scalar(svec[:, 4:6], svec[:, 0:2], 2.0, None, ALU.mult)
                v.tensor_scalar(svec[:, 6:7], svec[:, 2:3], 1.0, None, ALU.mult)

        def phase_P0(qh):
            q0 = qh * HALF
            bX, bX1 = sXG.n, sXG1.n
            bV, bA, bT, bE = sV.n, sA.n, sT.n, sEV.n

            def xg_wait(eng, tl):
                sm, b = (sXG, bX) if tl % 2 == 0 else (sXG1, bX1)
                eng.wait_ge(sm.sem, b + 16 * (tl // 2 + 1))
            pb3 = pbBF.rearrange("p (c n) -> p c n", n=128)
            with nc.Block() as blk:
                @blk.gpsimd
                def _(gp):
                    for tl in range(NT):
                        if tl > 1:
                            gp.wait_ge(sA.sem, bA + 3 * (tl - 1))
                        gp.dma_start(
                            out=xfBF[:, (tl % 2) * D:(tl % 2 + 1) * D],
                            in_=x_h[q0 + tl * 128:q0 + (tl + 1) * 128, :]
                        ).then_inc((sXG if tl % 2 == 0 else sXG1).sem, 16)

                @blk.vector
                def _(v):
                    for tl in range(NT):
                        xs = xfBF[:, (tl % 2) * D:(tl % 2 + 1) * D]
                        xg_wait(v, tl)
                        ln_sums_v(v, xs)
                        v.wait_ge(sA.sem, bA + 3 * tl + 1)
                        ln_var_v(v)
                        v.wait_ge(sA.sem, bA + 3 * tl + 2)
                        ln_v2(v)
                        for r in range(4):
                            v.wait_ge(sT.sem, bT + 4 * tl + r + 1)
                            v.tensor_copy(
                                A[:, r * 8:r * 8 + 8, tl * 128:(tl + 1) * 128],
                                pb3[:, 0:8, :]).then_inc(sEV.sem, 1)

                @blk.scalar
                def _(s):
                    for tl in range(NT):
                        xs = xfBF[:, (tl % 2) * D:(tl % 2 + 1) * D]
                        xg_wait(s, tl)
                        if tl == 0 and sOUT.n:
                            s.wait_ge(sOUT.sem, sOUT.n)
                        ln_sq_s(s, xs, ohBF[:, :])
                        if tl > 0:
                            s.wait_ge(sT.sem, bT + 4 * tl)
                        s.wait_ge(sV.sem, bV + 3 * tl + 2)
                        ln_s(s)
                        s.wait_ge(sV.sem, bV + 3 * tl + 3)
                        s.activation(ln[:, :], xs, AF.Identity,
                                     bias=st[:, 8:9], scale=st[:, 7:8]
                                     ).then_inc(sA.sem, 1)

                @blk.tensor
                def _(t):
                    for tl in range(NT):
                        t.wait_ge(sA.sem, bA + 3 * tl + 3)
                        for r in range(4):
                            if tl + r > 0:
                                t.wait_ge(sEV.sem, bE + 4 * tl + r)
                            for c in range(r * 8, r * 8 + 8):
                                i = t.transpose(
                                    pbBF[:, (c % 8) * 128:(c % 8 + 1) * 128],
                                    ln[:, c * 128:(c + 1) * 128], idt[:])
                            i.then_inc(sT.sem, 1)
            sXG.n = bX + 8 * NT
            sXG1.n = bX1 + 8 * NT
            sV.n = bV + 3 * NT
            sA.n = bA + 3 * NT
            sT.n = bT + 4 * NT
            sEV.n = bE + 4 * NT

        def emit_pf(gp, wsrc, widths, tw_end=None):
            if tw_end is None:
                tw_end = sTW.n
            bases = (sPF0.n, sPF1.n, tw_end)
            for n in range(2):
                if tw_end - 1 + n > 0:
                    gp.wait_ge(sTW.sem, tw_end - 1 + n)
                gp.dma_start(out=wst[:, n % 2, :, 0:widths[0]],
                             in_=wsrc(0, n)).then_inc((sPF0, sPF1)[n].sem, 16)
            (sPF0, sPF1)[n].n += 0
            sPF0.n += 16
            sPF1.n += 16
            return bases

        def slice_phase(n_slices, widths, wsrc, lhsT_get, affine,
                        evac_engine, evac_fn, pre_evac_wait=None,
                        pf_in=None, pf_next=None, extra_vec=None):
            if pf_in is not None:
                bWb = pf_in[:2]
                bTW = pf_in[2]
            else:
                bWb = (sW.n, sW1.n)
                bTW = sTW.n
            sWb = (sW, sW1)
            bAF, bE = sAF.n, sEV.n
            ngrp = 4 * n_slices
            pf_bases = []
            with nc.Block() as blk:
                @blk.gpsimd
                def _(gp):
                    for n in range(0 if pf_in is None else 2, ngrp):
                        if n >= 2:
                            gp.wait_ge(sTW.sem, bTW + n - 1)
                        gp.dma_start(out=wst[:, n % 2, :, 0:widths[n // 4]],
                                     in_=wsrc(n // 4, n % 4)
                                     ).then_inc(sWb[n % 2].sem, 16)
                    if pf_next is not None:
                        pf_bases.append(emit_pf(gp, pf_next[0], pf_next[1],
                                                tw_end=bTW + ngrp))

                if extra_vec is not None:
                    @blk.vector
                    def _(v):
                        extra_vec(v)

                if affine:
                    @blk.vector
                    def _(v):
                        for n in range(ngrp):
                            if pf_in is not None and n < 2:
                                v.wait_ge((sPF0, sPF1)[n].sem, pf_in[n] + 16)
                            else:
                                v.wait_ge(sWb[n % 2].sem,
                                          bWb[n % 2] + 16 * (n // 2 + 1))
                            Wd = widths[n // 4]
                            w = wst[:, n % 2]
                            if affine != "cs" or n % 4 < 3:
                                i = v.tensor_scalar(w[:, :, 0:Wd], w[:, :, 0:Wd],
                                                    2.0, -1.0, ALU.mult, ALU.add)
                            else:
                                v.tensor_scalar(w[:, 0:1, 0:Wd], w[:, 0:1, 0:Wd],
                                                2.0, -1.0, ALU.mult, ALU.add)
                                v.tensor_scalar(w[:, 1:2, 0:Wd], w[:, 1:2, 0:Wd],
                                                cs[:, NCH:NCH + 1],
                                                cs[:, NCH + 1:NCH + 2],
                                                ALU.mult, ALU.add)
                                i = v.tensor_scalar(w[:, 2:8, 0:Wd],
                                                    w[:, 2:8, 0:Wd],
                                                    -8.0, 4.0, ALU.mult, ALU.add)
                            i.then_inc(sAF.sem, 1)

                @blk.tensor
                def _(t):
                    for sl in range(n_slices):
                        Wd = widths[sl]
                        for gq in range(4):
                            n = sl * 4 + gq
                            if affine:
                                t.wait_ge(sAF.sem, bAF + n + 1)
                            elif pf_in is not None and n < 2:
                                t.wait_ge((sPF0, sPF1)[n].sem,
                                          pf_in[n] + 16)
                            else:
                                t.wait_ge(sWb[n % 2].sem,
                                          bWb[n % 2] + 16 * (n // 2 + 1))
                            for tl in range(NT):
                                if gq == 0 and sl > 0:
                                    t.wait_ge(sEV.sem,
                                              bE + (sl - 1) * 8 + tl + 1)
                                for ii in range(8):
                                    i = gq * 8 + ii
                                    mi = t.matmul(pb[tl][:, 0:Wd],
                                                  lhsT_get(i, tl),
                                                  wst[:, n % 2, ii, 0:Wd],
                                                  start=(i == 0),
                                                  stop=(i == 31))
                            mi.then_inc(sTW.sem, 1)

                eng_dec = blk.vector if evac_engine == "v" else blk.scalar

                @eng_dec
                def _(e):
                    if pre_evac_wait is not None:
                        pre_evac_wait(e)
                    for sl in range(n_slices):
                        e.wait_ge(sTW.sem, bTW + 4 * (sl + 1))
                        for tl in range(NT):
                            i = evac_fn(e, sl, tl)
                            i.then_inc(sEV.sem, 1)
            sW.n = bWb[0] + 16 * ((ngrp + 1) // 2 - (1 if pf_in is not None else 0))
            sW1.n = bWb[1] + 16 * (ngrp // 2 - (1 if pf_in is not None else 0))
            if affine:
                sAF.n = bAF + ngrp
            sTW.n = bTW + ngrp
            sEV.n = bE + 8 * n_slices
            return pf_bases[0] if pf_bases else None

        def phase_P1(qh, pf_in=None, pf_next=None):
            q0 = qh * HALF
            t0 = qh * NT
            widths = [512] * 8 + [256]

            def wsrc(sl, gq):
                Wd = widths[sl]
                src = wqkv_h[gq * 1024:(gq + 1) * 1024,
                             sl * 512:sl * 512 + Wd]
                return src.rearrange("(c p) n -> p c n", p=128)

            def evac(e, sl, tl):
                if sl < 8:
                    return e.tensor_copy(Bt[:, tl, sl * 512:(sl + 1) * 512],
                                         pb[tl][:, 0:512])
                e.tensor_scalar(ln[:, tl * 128:(tl + 1) * 128],
                                pb[tl][:, 0:128], SCALE, None, ALU.mult)
                return e.tensor_copy(Vb[:, t0 + tl, :], pb[tl][:, 128:256])

            pf = slice_phase(9, widths, wsrc,
                             lambda i, tl: A[:, i, tl * 128:(tl + 1) * 128],
                             affine=False, evac_engine="v", evac_fn=evac,
                             pf_in=pf_in, pf_next=pf_next)

            bT, bE = sT.n, sEV.n
            with nc.Block() as blk:
                @blk.tensor
                def _(t):
                    for tl in range(NT):
                        i = t.transpose(pbBF[:, tl * 128:(tl + 1) * 128],
                                        ln[:, tl * 128:(tl + 1) * 128], idt[:])
                    i.then_inc(sT.sem, 1)

                @blk.vector
                def _(v):
                    v.wait_ge(sT.sem, bT + 1)
                    v.tensor_copy(KT[:, q0:q0 + HALF], pbBF[:, 0:1024]
                                  ).then_inc(sEV.sem, 1)
            sT.n = bT + 1
            sEV.n = bE + 1
            return pf

        def phase_P2(qh, p2_pf=None):
            triples = []
            head_last = {}
            for h in range(NH):
                for qloc in range(2):
                    qq = qh * 2 + qloc
                    for kt in range(_ktmin(h, qq), 4 * qq + 4):
                        triples.append((h, qloc, kt))
                head_last[h] = len(triples) - 1
            ntrip = len(triples)
            diag_upto = []
            dc = 0
            for (h, qloc, kt) in triples:
                if kt >= 4 * (qh * 2 + qloc):
                    dc += 1
                diag_upto.append(dc)
            qi = {}
            for j, (h, qloc, kt) in enumerate(triples):
                if (h, qloc) not in qi:
                    qi[(h, qloc)] = qinst[0] + len(qi)
            q_last_j = {}
            for j, (h, qloc, kt) in enumerate(triples):
                q_last_j[(h, qloc)] = j

            bQr, bQf, bST, bMSK, bEXP, bPV, bT, bES = (
                sQr.n, sQf.n, sST.n, sMSK.n, sEXP.n, sPV.n, sT.n, sES.n)
            qi0 = qinst[0]

            def trip_parts(j):
                h, qloc, kt = triples[j]
                qq = qh * 2 + qloc
                qs, Wd, subs = _subtiles(h, qq, kt)
                n = qi[(h, qloc)]
                return h, qloc, kt, qq, qs, Wd, subs, n

            def emit_dpv(t, j):
                h, qloc, kt, qq, qs, Wd, subs, n = trip_parts(j)
                nkt = 4 * qq + 4
                km = _ktmin(h, qq)
                t.wait_ge(sEXP.sem, bEXP + j + 1)
                di = t.matmul(pb[2 + n % 2][:, qs:512], on1[:, :],
                              PTs[:, j % 4, 0:Wd],
                              start=(kt == km), stop=(kt == nkt - 1))
                if j == head_last[h]:
                    di.then_inc(sQf.sem, 1)
                mi = t.matmul(pb[n % 2][:, qs:512], Vb[:, kt, :],
                              PTs[:, j % 4, 0:Wd],
                              start=(kt == km), stop=(kt == nkt - 1))
                mi.then_inc(sPV.sem, 1)

            with nc.Block() as blk:
                if p2_pf is not None:
                    pf_out2 = []

                    @blk.gpsimd
                    def _(gp):
                        pf_out2.append(emit_pf(gp, p2_pf[0], p2_pf[1]))

                @blk.tensor
                def _(t):
                    j = 0
                    for h in range(NH):
                        if h >= 1:
                            t.wait_ge(sQr.sem, bQr + h)
                        for tl in range(NT):
                            i = t.transpose(
                                pbBF[:, tl * 128:(tl + 1) * 128],
                                Bt[:, tl, h * 128:(h + 1) * 128], idt[:])
                        i.then_inc(sT.sem, 1)
                        jh0 = j
                        for qloc in range(2):
                            qq = qh * 2 + qloc
                            nkt = 4 * qq + 4
                            for kt in range(_ktmin(h, qq), nkt):
                                _, _, _, _, qs, Wd, subs, n = trip_parts(j)
                                if j >= 3:
                                    t.wait_ge(sEXP.sem, bEXP + j - 2)
                                if qloc == 0 and kt == _ktmin(h, qq):
                                    t.wait_ge(sQr.sem, bQr + h + 1)
                                if kt == _ktmin(h, qq) and n >= 2:
                                    t.wait_ge(sEVq.sem, 2 * (n - 1))
                                diag = kt >= 4 * qq
                                mi = t.matmul(
                                    pb[4 + j % 3][:, 0:Wd],
                                    KT[:, kt * 128:(kt + 1) * 128],
                                    QT[:, h % 2,
                                       qloc * 512 + qs:qloc * 512 + 512],
                                    start=True, stop=not diag)
                                if diag:
                                    mi = t.matmul(pb[4 + j % 3][:, 0:128],
                                                  idt[:], tri[:],
                                                  start=False, stop=True)
                                mi.then_inc(sST.sem, 1)
                                if j > jh0 + 1:
                                    emit_dpv(t, j - 2)
                                j += 1
                        if j > jh0 + 1:
                            emit_dpv(t, j - 2)
                        emit_dpv(t, j - 1)

                @blk.vector
                def _(v):
                    def evac(hh, qloc):
                        n = qi[(hh, qloc)]
                        v.wait_ge(sPV.sem, bPV + q_last_j[(hh, qloc)] + 1)
                        if n >= 1:
                            v.wait_ge(sEVq.sem, 2 * n)
                        v.reciprocal(DR[:, :], pb[2 + n % 2][:, :]
                                     ).then_inc(sEVq.sem, 1)
                        v.wait_ge(sEVq.sem, 2 * n + 1)
                        v.tensor_tensor(
                            A[:, hh, qloc * 512:(qloc + 1) * 512],
                            pb[n % 2][:, :], DR[:, :], ALU.mult
                        ).then_inc(sEVq.sem, 1)

                    for h in range(NH):
                        v.wait_ge(sT.sem, bT + h + 1)
                        if h >= 2:
                            v.wait_ge(sQf.sem, bQf + h - 1)
                        v.tensor_copy(QT[:, h % 2, :], pbBF[:, 0:1024]
                                      ).then_inc(sQr.sem, 1)
                        if h >= 1:
                            evac(h - 1, 1)
                        evac(h, 0)
                    evac(NH - 1, 1)

                @blk.scalar
                def _(s):
                    j = 0
                    for h in range(NH):
                        for qloc in range(2):
                            qq = qh * 2 + qloc
                            nkt = 4 * qq + 4
                            for kt in range(_ktmin(h, qq), nkt):
                                _, _, _, _, qs, Wd, subs, n = trip_parts(j)
                                s.wait_ge(sST.sem, bST + j + 1)
                                if j >= 4:
                                    s.wait_ge(sPV.sem, bPV + j - 3)
                                for (off, w, delta) in subs:
                                    col = EB_COLS[(h, delta)]
                                    i = s.activation(
                                        PTs[:, j % 4, off:off + w],
                                        pb[4 + j % 3][:, off:off + w],
                                        AF.Exp, bias=eb[:, col:col + 1])
                                i.then_inc(sEXP.sem, 1)
                                j += 1

            sQr.n = bQr + NH
            sQf.n = bQf + NH
            sST.n = bST + ntrip
            sEXP.n = bEXP + ntrip
            sPV.n = bPV + ntrip
            sT.n = bT + NH
            qinst[0] += 2 * NH
            sEVq.n = 2 * qinst[0]
            return pf_out2[0] if p2_pf is not None else None

        def phase_P3(qh, pf_in=None, pf_next=None):
            widths = [512] * 8

            def wsrc(sl, gq):
                src = wo_h[gq * 1024:(gq + 1) * 1024,
                           sl * 512:(sl + 1) * 512]
                return src.rearrange("(c p) n -> p c n", p=128)

            def evac(e, sl, tl):
                return e.activation(Bt[:, tl, sl * 512:(sl + 1) * 512],
                                    pb[tl][:, 0:512], AF.Identity,
                                    accum_out=st[:, 16 + 8 * tl + sl:
                                                 17 + 8 * tl + sl])

            bE3 = sEV.n

            def extra_vec(v):
                for sl in range(8):
                    for tl in range(NT):
                        v.wait_ge(sEV.sem, bE3 + 8 * sl + tl + 1)
                        b = Bt[:, tl, sl * 512:(sl + 1) * 512]
                        v.scalar_tensor_tensor(
                            oh[:, 512:1024], b, 1.0, b, ALU.mult, ALU.mult,
                            accum_out=st[:, 80 + 8 * tl + sl:
                                         81 + 8 * tl + sl])

            return slice_phase(8, widths, wsrc,
                               lambda i, tl: A[:, i, tl * 128:(tl + 1) * 128],
                               affine=False, evac_engine="s", evac_fn=evac,
                               pf_in=pf_in, pf_next=pf_next,
                               extra_vec=extra_vec)

        def phase_P35(qh, p35_pf=None):
            q0 = qh * HALF
            bX, bX1 = sXG.n, sXG1.n
            bV, bA, bT, bE, bSP = sV.n, sA.n, sT.n, sEV.n, sSP.n

            def xg_wait(eng, tl):
                sm, b = (sXG, bX) if tl % 2 == 0 else (sXG1, bX1)
                eng.wait_ge(sm.sem, b + 16 * (tl // 2 + 1))
            pb3 = pbBF.rearrange("p (c n) -> p c n", n=128)
            pf_out = []
            with nc.Block() as blk:
                @blk.gpsimd
                def _(gp):
                    if p35_pf is not None:
                        pf_out.append(emit_pf(gp, p35_pf[0], p35_pf[1]))
                    for tl in range(NT):
                        if tl > 1:
                            gp.wait_ge(sV.sem, bV + 7 * (tl - 2) + 4)
                        gp.dma_start(
                            out=xfBF[:, (tl % 2) * D:(tl % 2 + 1) * D],
                            in_=x_h[q0 + tl * 128:q0 + (tl + 1) * 128, :]
                        ).then_inc((sXG if tl % 2 == 0 else sXG1).sem, 16)

                @blk.sync
                def _(sy):
                    for tl in range(NT):
                        sy.wait_ge(sV.sem, bV + 7 * tl + 4)
                        sy.dma_start(
                            out=x1sp[q0 + tl * 128:q0 + (tl + 1) * 128, :],
                            in_=Bt[:, tl, :]).then_inc(sSP.sem, 16)

                @blk.vector
                def _(v):
                    for tl in range(NT):
                        v.reduce_sum(st[:, 0:1],
                                     st[:, 16 + 8 * tl:24 + 8 * tl],
                                     axis=AX.X)
                        v.reduce_sum(st[:, 1:2],
                                     st[:, 80 + 8 * tl:88 + 8 * tl],
                                     axis=AX.X).then_inc(sV.sem, 1)
                        ln_var_v(v)
                        v.wait_ge(sA.sem, bA + 5 * tl + 1)
                        ln_v2(v)
                        v.wait_ge(sA.sem, bA + 5 * tl + 2)
                        xg_wait(v, tl)
                        v.tensor_tensor(Bt[:, tl, :],
                                        xfBF[:, (tl % 2) * D:(tl % 2 + 1) * D],
                                        ln[:, :], ALU.add
                                        ).then_inc(sV.sem, 1)
                        v.drain()
                        ln_sums_v(v, Bt[:, tl, :])
                        v.wait_ge(sA.sem, bA + 5 * tl + 3)
                        ln_var_v(v)
                        v.wait_ge(sA.sem, bA + 5 * tl + 4)
                        ln_v2(v)
                        for r in range(4):
                            v.wait_ge(sT.sem, bT + 4 * tl + r + 1)
                            v.tensor_copy(
                                A[:, r * 8:r * 8 + 8, tl * 128:(tl + 1) * 128],
                                pb3[:, 0:8, :]).then_inc(sEV.sem, 1)

                @blk.scalar
                def _(s):
                    for tl in range(NT):
                        if tl > 0:
                            s.wait_ge(sT.sem, bT + 4 * tl)
                        s.wait_ge(sV.sem, bV + 7 * tl + 2)
                        ln_s(s)
                        s.wait_ge(sV.sem, bV + 7 * tl + 3)
                        s.activation(ln[:, :], Bt[:, tl, :], AF.Identity,
                                     bias=st[:, 8:9], scale=st[:, 7:8]
                                     ).then_inc(sA.sem, 1)
                        s.wait_ge(sV.sem, bV + 7 * tl + 4)
                        if tl == 0 and sOUT.n:
                            s.wait_ge(sOUT.sem, sOUT.n)
                        ln_sq_s(s, Bt[:, tl, :], ohBF[:, :])
                        s.wait_ge(sV.sem, bV + 7 * tl + 6)
                        ln_s(s)
                        s.wait_ge(sV.sem, bV + 7 * tl + 7)
                        s.activation(ln[:, :], Bt[:, tl, :], AF.Identity,
                                     bias=st[:, 8:9], scale=st[:, 7:8]
                                     ).then_inc(sA.sem, 1)

                @blk.tensor
                def _(t):
                    for tl in range(NT):
                        t.wait_ge(sA.sem, bA + 5 * tl + 5)
                        for r in range(4):
                            if tl + r > 0:
                                t.wait_ge(sEV.sem, bE + 4 * tl + r)
                            for c in range(r * 8, r * 8 + 8):
                                i = t.transpose(
                                    pbBF[:, (c % 8) * 128:(c % 8 + 1) * 128],
                                    ln[:, c * 128:(c + 1) * 128], idt[:])
                            i.then_inc(sT.sem, 1)
            sXG.n = bX + 8 * NT
            sXG1.n = bX1 + 8 * NT
            sV.n = bV + 7 * NT
            sA.n = bA + 5 * NT
            sT.n = bT + 4 * NT
            sEV.n = bE + 4 * NT
            sSP.n = bSP + 16 * NT

        def arena_chunk(arena, c, w=HALF):
            if arena is A:
                return A[:, c, 0:w]
            return Bt[:, c // 4, (c % 4) * HALF:(c % 4) * HALF + w]

        def ffn_fm_phase(qh, layer, src_arena, dst_arena, colscale,
                         pre_evac_wait=None, pf_in=None, pf_next=None):
            kh = k_h[layer]
            scol = (4, 5, 6)[layer]
            func = AF.Relu if layer < 2 else AF.Identity
            if pf_in is not None:
                bWb = pf_in[:2]
                bTW = pf_in[2]
            else:
                bWb = (sW.n, sW1.n)
                bTW = sTW.n
            sWb = (sW, sW1)
            bAF, bE = sAF.n, sEV.n
            npair = 16
            ngrp = 4 * npair
            pf_bases = []

            def wsrc(p, g):
                return kh[g * 1024:(g + 1) * 1024,
                          p * 256:(p + 1) * 256
                          ].rearrange("(c p) n -> p c n", p=128)

            with nc.Block() as blk:
                @blk.gpsimd
                def _(gp):
                    for n in range(0 if pf_in is None else 2, ngrp):
                        if n >= 2:
                            gp.wait_ge(sTW.sem, bTW + n - 1)
                        gp.dma_start(out=wst[:, n % 2, :, 0:256],
                                     in_=wsrc(n // 4, n % 4)
                                     ).then_inc(sWb[n % 2].sem, 16)
                    if pf_next is not None:
                        pf_bases.append(emit_pf(gp, pf_next[0], pf_next[1],
                                                tw_end=bTW + ngrp))

                @blk.vector
                def _(v):
                    for n in range(ngrp):
                        if pf_in is not None and n < 2:
                            v.wait_ge((sPF0, sPF1)[n].sem, pf_in[n] + 16)
                        else:
                            v.wait_ge(sWb[n % 2].sem,
                                      bWb[n % 2] + 16 * (n // 2 + 1))
                        g = n % 4
                        w = wst[:, n % 2]
                        if (not colscale) or g < 3:
                            i = v.tensor_scalar(w[:, :, 0:256], w[:, :, 0:256],
                                                2.0, -1.0, ALU.mult, ALU.add)
                        else:
                            v.tensor_scalar(w[:, 0:1, 0:256], w[:, 0:1, 0:256],
                                            2.0, -1.0, ALU.mult, ALU.add)
                            v.tensor_scalar(w[:, 1:2, 0:256], w[:, 1:2, 0:256],
                                            cs[:, NCH:NCH + 1],
                                            cs[:, NCH + 1:NCH + 2],
                                            ALU.mult, ALU.add)
                            i = v.tensor_scalar(w[:, 2:8, 0:256],
                                                w[:, 2:8, 0:256],
                                                -8.0, 4.0, ALU.mult, ALU.add)
                        i.then_inc(sAF.sem, 1)

                @blk.tensor
                def _(t):
                    for p in range(npair):
                        pbase = 4 * (p % 2)
                        for g in range(4):
                            n = p * 4 + g
                            t.wait_ge(sAF.sem, bAF + n + 1)
                            if g == 0 and p >= 2:
                                t.wait_ge(sEV.sem, bE + 2 * (p - 1))
                            for ii in range(8):
                                i = g * 8 + ii
                                for oc in range(2):
                                    for sh in range(2):
                                        mi = t.matmul(
                                            pb[pbase + 2 * oc + sh][:, 0:512],
                                            wst[:, n % 2, ii,
                                                oc * 128:oc * 128 + 128],
                                            arena_chunk(src_arena, i
                                                        )[:, sh * 512:
                                                          sh * 512 + 512],
                                            start=(i == 0), stop=(i == 31))
                            mi.then_inc(sTW.sem, 1)

                @blk.scalar
                def _(s):
                    if pre_evac_wait is not None:
                        pre_evac_wait(s)
                    for p in range(npair):
                        pbase = 4 * (p % 2)
                        s.wait_ge(sTW.sem, bTW + 4 * (p + 1))
                        for oc in range(2):
                            dst = arena_chunk(dst_arena, 2 * p + oc)
                            ps2 = pb[pbase + 2 * oc].ap()
                            s.activation(dst[:, 0:512], ps2[:, 0:512], func,
                                         scale=svec[:, scol:scol + 1])
                            s.activation(dst[:, 512:1024],
                                         pb[pbase + 2 * oc + 1][:, 0:512],
                                         func, scale=svec[:, scol:scol + 1]
                                         ).then_inc(sEV.sem, 1)

            sW.n = bWb[0] + 16 * ((ngrp + 1) // 2 - (1 if pf_in is not None else 0))
            sW1.n = bWb[1] + 16 * (ngrp // 2 - (1 if pf_in is not None else 0))
            sAF.n = bAF + ngrp
            sTW.n = bTW + ngrp
            sEV.n = bE + 2 * npair
            return pf_bases[0] if pf_bases else None

        def p4_wsrc(layer):
            def wsrc(sl, gq):
                src = k_h[layer][gq * 1024:(gq + 1) * 1024,
                                 sl * 512:(sl + 1) * 512]
                return src.rearrange("(c p) n -> p c n", p=128)
            return wsrc

        def phase_P4c(qh, pf_in=None, pf_next=None):
            def evac(e, sl, tl):
                return e.activation(Bt[:, tl, sl * 512:(sl + 1) * 512],
                                    pb[tl][:, 0:512], AF.Identity,
                                    scale=svec[:, 6:7])

            return slice_phase(8, [512] * 8, p4_wsrc(2),
                               lambda i, tl: A[:, i, tl * 128:(tl + 1) * 128],
                               affine="cs", evac_engine="s", evac_fn=evac,
                               pf_in=pf_in, pf_next=pf_next)

        def phase_P4T(qh):
            bT, bE = sT.n, sEV.n
            pb3 = pbBF.rearrange("p (c n) -> p c n", n=128)
            with nc.Block() as blk:
                @blk.tensor
                def _(t):
                    for tl in range(NT):
                        for r in range(4):
                            if tl + r > 0:
                                t.wait_ge(sEV.sem, bE + 4 * tl + r)
                            for c in range(r * 8, r * 8 + 8):
                                i = t.transpose(
                                    pbBF[:, (c % 8) * 128:(c % 8 + 1) * 128],
                                    Bt[:, tl, c * 128:(c + 1) * 128], idt[:])
                            i.then_inc(sT.sem, 1)

                @blk.vector
                def _(v):
                    for tl in range(NT):
                        dst = lambda lo, hi: A[:, lo:hi, tl * 128:(tl + 1) * 128]
                        for r in range(4):
                            v.wait_ge(sT.sem, bT + 4 * tl + r + 1)
                            if r < 3:
                                i = v.tensor_copy(dst(r * 8, r * 8 + 8),
                                                  pb3[:, 0:8, :])
                            else:
                                v.tensor_copy(dst(24, 25), pb3[:, 0:1, :])
                                v.tensor_scalar(dst(25, 26), pb3[:, 1:2, :],
                                                cs[:, 25:26], None, ALU.mult)
                                i = v.tensor_scalar(dst(26, 32), pb3[:, 2:8, :],
                                                    EI, None, ALU.mult)
                            i.then_inc(sEV.sem, 1)
            sT.n = bT + 4 * NT
            sEV.n = bE + 4 * NT

        def phase_P5(qh, p5_pf=None):
            q0 = qh * HALF
            bV, bA, bX1, bO = sV.n, sA.n, sX1.n, sOUT.n
            pf_out = []
            with nc.Block() as blk:
                if p5_pf is not None:
                    @blk.gpsimd
                    def _(gp):
                        pf_out.append(emit_pf(gp, p5_pf[0], p5_pf[1]))
                @blk.sync
                def _(sy):
                    sy.wait_ge(sSP.sem, sSP.n)
                    for tl in range(NT):
                        row = q0 + tl * 128
                        if tl > 0:
                            sy.wait_ge(sV.sem, bV + 5 * tl)
                        sy.dma_start(
                            out=ln[:, 2048:4096],
                            in_=x1sp[row:row + 128, 0:2048]
                        ).then_inc(sX1.sem, 16)
                        sy.wait_ge(sV.sem, bV + 5 * tl + 4)
                        sy.dma_start(
                            out=ln[:, 2048:4096],
                            in_=x1sp[row:row + 128, 2048:4096]
                        ).then_inc(sX1.sem, 16)
                        sy.dma_start(
                            out=out_h[row:row + 128, 0:2048],
                            in_=oh[:, :]).then_inc(sOUT.sem, 16)
                        sy.wait_ge(sV.sem, bV + 5 * tl + 5)
                        sy.dma_start(
                            out=out_h[row:row + 128, 2048:4096],
                            in_=oh[:, :]).then_inc(sOUT.sem, 16)

                @blk.vector
                def _(v):
                    for tl in range(NT):
                        if tl > 0:
                            v.drain()
                        ln_sums_v(v, Bt[:, tl, :])
                        v.wait_ge(sA.sem, bA + 4 * tl + 1)
                        ln_var_v(v)
                        v.wait_ge(sA.sem, bA + 4 * tl + 2)
                        ln_v2(v)
                        for jj in range(2):
                            v.wait_ge(sX1.sem, bX1 + 16 * (2 * tl + jj + 1))
                            v.wait_ge(sA.sem, bA + 4 * tl + 3 + jj)
                            if jj == 1:
                                v.wait_ge(sOUT.sem, bO + 32 * tl + 16)
                            else:
                                v.wait_ge(sOUT.sem, bO + 32 * tl)
                            v.tensor_tensor(oh[:, :], ln[:, 2048:4096],
                                            ln[:, 0:2048], ALU.add
                                            ).then_inc(sV.sem, 1)

                @blk.scalar
                def _(s):
                    for tl in range(NT):
                        if tl > 0:
                            s.wait_ge(sV.sem, bV + 5 * tl)
                        ln_sq_s(s, Bt[:, tl, :])
                        s.wait_ge(sV.sem, bV + 5 * tl + 2)
                        ln_s(s)
                        s.wait_ge(sV.sem, bV + 5 * tl + 3)
                        s.activation(ln[:, 0:2048], Bt[:, tl, 0:2048],
                                     AF.Identity, bias=st[:, 8:9],
                                     scale=st[:, 7:8]).then_inc(sA.sem, 1)
                        s.wait_ge(sV.sem, bV + 5 * tl + 4)
                        s.activation(ln[:, 0:2048], Bt[:, tl, 2048:4096],
                                     AF.Identity, bias=st[:, 8:9],
                                     scale=st[:, 7:8]).then_inc(sA.sem, 1)
            sV.n = bV + 5 * NT
            sA.n = bA + 4 * NT
            sX1.n = bX1 + 32 * NT
            sOUT.n = bO + 32 * NT
            return pf_out[0] if p5_pf is not None else None

        P = _PHASES
        p1_w = (lambda sl, gq: wqkv_h[gq * 1024:(gq + 1) * 1024,
                                      sl * 512:sl * 512 + ([512] * 8 + [256])[sl]
                                      ].rearrange("(c p) n -> p c n", p=128),
                [512] * 8 + [256])
        p3_w = (lambda sl, gq: wo_h[gq * 1024:(gq + 1) * 1024,
                                    sl * 512:(sl + 1) * 512
                                    ].rearrange("(c p) n -> p c n", p=128),
                [512] * 8)
        p4a_w = (lambda p, g: k_h[0][g * 1024:(g + 1) * 1024,
                                     p * 256:(p + 1) * 256
                                     ].rearrange("(c p) n -> p c n", p=128),
                 [256] * 16)
        p4b_w = (lambda p, g: k_h[1][g * 1024:(g + 1) * 1024,
                                     p * 256:(p + 1) * 256
                                     ].rearrange("(c p) n -> p c n", p=128),
                 [256] * 16)
        p4c_w = (lambda sl, gq: k_h[2][gq * 1024:(gq + 1) * 1024,
                                       sl * 512:(sl + 1) * 512
                                       ].rearrange("(c p) n -> p c n", p=128),
                 [512] * 8)

        pf1 = [None]
        if _PF:
            with nc.Block() as blk:
                @blk.gpsimd
                def _(gp):
                    pf1[0] = emit_pf(gp, p1_w[0], p1_w[1])

        pf = pf1[0]
        for qh in range(_HALVES):
            phase_P0(qh)
            if qh == 0:
                tap_dump("xnT", A[:, :, :].rearrange("p c n -> p (c n)"))
            pf = phase_P1(qh, pf_in=pf, pf_next=None)
            if qh == 0:
                tap_dump("q", Bt[:, :, :].rearrange("p t n -> p (t n)"))
                tap_dump("kt", KT[:, :])
                tap_dump("v", Vb[:, :, :].rearrange("p t n -> p (t n)"))
            pf = phase_P2(qh, p2_pf=(p3_w if _PF else None))
            if qh == 0:
                tap_dump("attnT", A[:, :, :].rearrange("p c n -> p (c n)"))
            phase_P3(qh, pf_in=pf, pf_next=None)
            if qh == 0:
                tap_dump("y", Bt[:, :, :].rearrange("p t n -> p (t n)"))
            pf = phase_P35(qh, p35_pf=(p4a_w if _PF else None))
            if qh == 0:
                tap_dump("x1", Bt[:, :, :].rearrange("p t n -> p (t n)"))
                tap_dump("znT", A[:, :, :].rearrange("p c n -> p (c n)"))
            spn = sSP.n

            def spw(e, spn=spn):
                e.wait_ge(sSP.sem, spn)

            pf = ffn_fm_phase(qh, 0, A, Bt, colscale=False,
                              pre_evac_wait=spw, pf_in=pf,
                              pf_next=(p4b_w if _PF else None))
            if qh == 0:
                tap_dump("h0T", Bt[:, :, :].rearrange("p t n -> p (t n)"))
            pf = ffn_fm_phase(qh, 1, Bt, A, colscale=True,
                              pf_in=pf, pf_next=(p4c_w if _PF else None))
            if qh == 0:
                tap_dump("h1T", A[:, :, :].rearrange("p c n -> p (c n)"))
            phase_P4c(qh, pf_in=pf, pf_next=None)
            if qh == 0:
                tap_dump("f", Bt[:, :, :].rearrange("p t n -> p (t n)"))
            pf = phase_P5(qh, p5_pf=(p1_w if qh == 0 and _PF else None))

        with nc.Block() as blk:
            @blk.gpsimd
            def _(gp):
                gp.wait_ge(sOUT.sem, sOUT.n)
    return nc


def _consts():
    ident = np.eye(128, dtype=ml_dtypes.bfloat16)
    ones128 = np.ones((128, 128), dtype=ml_dtypes.bfloat16)
    onesrow = np.ones((1, 128), dtype=np.float32)
    pp = np.arange(128)
    tri = np.where(pp[:, None] > pp[None, :], NEG, 0.0).astype(ml_dtypes.bfloat16)
    f = (np.arange(NCH)[None, :] * 128 + pp[:, None])
    cs = np.where(f < NE, np.float32(1.0), np.float32(EI)).astype(np.float32)
    cs = np.concatenate([cs, 2.0 * cs[:, 25:26], -cs[:, 25:26]],
                        axis=1).astype(np.float32)
    return ident, ones128, onesrow, tri, cs


def kernel(L0_attn_hidden_states, L0_attn_Wqkv, L0_attn_Wo,
           L0_ffn_k0, L0_ffn_s0, L0_ffn_k1, L0_ffn_s1, L0_ffn_k2, L0_ffn_s2):
    x = np.ascontiguousarray(np.asarray(L0_attn_hidden_states, dtype=np.float32))
    wqkv = np.ascontiguousarray(np.asarray(L0_attn_Wqkv, dtype=np.float32))
    wo = np.ascontiguousarray(np.asarray(L0_attn_Wo, dtype=np.float32))
    ks = [np.ascontiguousarray(np.asarray(k).view(np.uint8))
          for k in (L0_ffn_k0, L0_ffn_k1, L0_ffn_k2)]
    sv = np.zeros((1, 4), dtype=np.float32)
    sv[0, 0] = np.float32(L0_ffn_s0)
    sv[0, 1] = np.float32(L0_ffn_s1)
    sv[0, 2] = np.float32(L0_ffn_s2)
    ident, ones128, onesrow, tri, cs = _consts()

    nc = build_program()
    base = dict(wqkv=wqkv, wo=wo, k0=ks[0], k1=ks[1], k2=ks[2], svals=sv,
                ident=ident, ones128=ones128, onesrow=onesrow, tri=tri,
                cs=cs, ebias=EB_TAB)
    in_maps = [dict(base, x=np.ascontiguousarray(x[c])) for c in range(8)]
    res = run_bass_kernel_spmd(nc, in_maps, core_ids=list(range(8)))
    out = np.stack([r["out"] for r in res.results], axis=0)
    return out.astype(np.float32)
